# version 15
# baseline (speedup 1.0000x reference)
import numpy as np
import ml_dtypes
from contextlib import ExitStack

import concourse.bass as bass
import concourse.mybir as mybir
from concourse.bass_utils import run_bass_kernel_spmd

F32 = mybir.dt.float32
BF16 = mybir.dt.bfloat16
I32 = mybir.dt.int32
AF = mybir.ActivationFunctionType
OP = mybir.AluOpType
AX = mybir.AxisListType

T, D, S = 4096, 2048, 64
TT = 128
NT = T // TT
NCH = D // 128
NW = D // 128
MAGIC = 0x5F3759DF
INV_D = 1.0 / D
INV_WN = 1.0 / (D * D)


def build():
    nc = bass.Bass("TRN2", target_bir_lowering=False)
    x_d = nc.dram_tensor("x", [T, D], F32, kind="ExternalInput").ap()
    wa_d = nc.dram_tensor("wa", [S, D], F32, kind="ExternalInput").ap()
    wb_d = nc.dram_tensor("wb", [S, D], F32, kind="ExternalInput").ap()
    ba_d = nc.dram_tensor("ba", [1, S], F32, kind="ExternalInput").ap()
    bb_d = nc.dram_tensor("bb", [1, S], F32, kind="ExternalInput").ap()
    wo_d = nc.dram_tensor("wout", [D, S], F32, kind="ExternalInput").ap()
    bo_d = nc.dram_tensor("bout16", [16, 128], F32, kind="ExternalInput").ap()
    bi_d = nc.dram_tensor("bias16", [16, 128], F32, kind="ExternalInput").ap()
    wf_d = nc.dram_tensor("wf", [D, D], F32, kind="ExternalInput").ap()
    idn_d = nc.dram_tensor("idn", [128, 128], BF16, kind="ExternalInput").ap()
    onr_d = nc.dram_tensor("onesr", [1, 128], BF16, kind="ExternalInput").ap()
    y_d = nc.dram_tensor("y", [T, D], F32, kind="ExternalOutput").ap()

    with ExitStack() as est:
        e = est.enter_context

        def sb(name, shape, dt):
            return e(nc.sbuf_tensor(name, shape, dt))

        def sem(name):
            return e(nc.semaphore(name))

        w_rhs = sb("w_rhs", [128, NCH * D], BF16)
        wab = sb("wab", [128, NCH * 128], BF16)
        wocv = sb("wocv", [65, D], BF16)
        hsbe = sb("hsbe", [65, TT], BF16)
        babr = sb("babr", [1, 128], BF16)
        sidn = sb("sidn", [128, 128], BF16)
        sonr = sb("sonr", [1, 128], BF16)
        sx = sb("sx", [128, 2 * D], F32)
        xnb = sb("xnb", [128, 2 * D], BF16)
        xnT = sb("xnT", [128, 2 * D], BF16)
        outb = sb("outb", [128, 2 * D], F32)
        vv = sb("vv", [128, 2 * D], F32)
        sqscr = sb("sqscr", [128, D], F32)
        sa = sb("sa", [64, TT], F32)
        sb0 = sb("sb0", [64, TT], F32)
        hs = sb("hs", [64, TT], F32)
        h_in = sb("h_in", [64, 1], F32)
        sums = sb("sums", [128, 2], F32)
        sqs = sb("sqs", [128, 2], F32)
        mm_ = sb("mm_", [128, 1], F32)
        msq = sb("msq", [128, 1], F32)
        uu = sb("uu", [128, 1], F32)
        t0s = sb("t0s", [128, 1], F32)
        yr = sb("yr", [128, 2], F32)
        p1r = sb("p1r", [128, 2], F32)
        nmr = sb("nmr", [128, 2], F32)
        magic = sb("magic", [128, 1], I32)
        two_i = sb("two_i", [128, 1], I32)
        shi = sb("shi", [128, 1], I32)
        thr128 = sb("thr128", [128, 1], F32)
        nthr128 = sb("nthr128", [128, 1], F32)
        negs128 = sb("negs128", [128, 1], F32)
        invs64 = sb("invs64", [64, 1], F32)
        onescol = sb("onescol", [128, 1], F32)
        onescolb = sb("onescolb", [128, 1], BF16)
        onesrf = sb("onesrf", [1, 128], F32)
        scale_s = sb("scale_s", [1, 1], F32)
        thr0 = sb("thr0", [1, 1], F32)
        nthr0 = sb("nthr0", [1, 1], F32)
        negs0 = sb("negs0", [1, 1], F32)
        invs0 = sb("invs0", [1, 1], F32)
        wasb = sb("wasb", [64, D], F32)
        wabb = sb("wabb", [64, D], BF16)
        basb = sb("basb", [1, S], F32)
        bbsb = sb("bbsb", [1, S], F32)
        wosb = sb("wosb", [128, NW * 64], F32)
        wob = sb("wob", [128, NW * 64], BF16)
        bo16 = sb("bo16", [16, 128], F32)
        bi16 = sb("bi16", [16, 128], F32)
        cneg = sb("cneg", [16, 128], F32)
        wtile = sb("wtile", [128, 2 * D], F32)
        wpart = sb("wpart", [128, 16], F32)
        acc = sb("acc", [128, 2], F32)
        g_sb = sb("g_sb", [128, D], F32)
        wbn = sb("wbn", [128, 2 * D], BF16)
        pffn = e(nc.psum_tensor("pffn", [128, D], F32))
        ptr = e(nc.psum_tensor("ptr", [128, D], BF16))
        pab = e(nc.psum_tensor("pab", [128, 128], F32))
        spare = e(nc.psum_tensor("spare", [128, 512], F32))
        s_x0, s_x1 = sem("s_x0"), sem("s_x1")
        s_st0, s_st1 = sem("s_st0"), sem("s_st1")
        s_wl0, s_wl1 = sem("s_wl0"), sem("s_wl1")
        s_wlb0, s_wlb1 = sem("s_wlb0"), sem("s_wlb1")
        s_wl2, s_wl3 = sem("s_wl2"), sem("s_wl3")
        s_wlb2 = sem("s_wlb2")
        s_sgn2 = sem("s_sgn2")
        s_pam = sem("s_pam")
        s_acp = sem("s_acp")
        s_small = sem("s_small")
        s_wbl = sem("s_wbl")
        s_wol = sem("s_wol")
        s_cvr = sem("s_cvr")
        s_cast = sem("s_cast")
        s_wabtr = sem("s_wabtr")
        s_wabcp = sem("s_wabcp")
        s_bab = sem("s_bab")
        s_woc = sem("s_woc")
        s_wotr = sem("s_wotr")
        s_wocp = sem("s_wocp")
        s_wabs = sem("s_wabs")
        s_acc = sem("s_acc")
        s_psum1 = sem("s_psum1")
        s_sclv = sem("s_sclv")
        s_bc = sem("s_bc")
        s_thr = sem("s_thr")
        s_cvecv = sem("s_cvecv")
        s_tern = sem("s_tern")
        s_wtr = sem("s_wtr")
        s_wtrc = sem("s_wtrc")
        s_hrow = sem("s_hrow")
        s_sums = sem("s_sums")
        s_stats = sem("s_stats")
        s_xnbs = sem("s_xnbs")
        s_tr = sem("s_tr")
        s_xnT = sem("s_xnT")
        s_ab = sem("s_ab")
        s_tanh = sem("s_tanh")
        s_scan = sem("s_scan")
        s_hsb = sem("s_hsb")
        s_mix = sem("s_mix")
        s_ev = sem("s_ev")
        s_ffn = sem("s_ffn")
        s_v = sem("s_v")
        s_x = [s_x0, s_x1]
        s_st = [s_st0, s_st1]
        s_wl = [s_wl0, s_wl1, s_wl2, s_wl3]
        s_wlb = [s_wlb0, s_wlb1, s_wlb2]
        block = e(nc.Block())

        N_SMALL = 7 * 16

        @block.sync
        def _(sync):
            sync.dma_start(out=sidn[:, :], in_=idn_d[:, :]).then_inc(s_small, 16)
            sync.dma_start(out=sonr[:, :], in_=onr_d[:, :]).then_inc(s_small, 16)
            sync.dma_start(out=wasb[:, :], in_=wa_d[:, :]).then_inc(s_small, 16)
            sync.dma_start(out=basb[:, :], in_=ba_d[:, :]).then_inc(s_small, 16)
            sync.dma_start(out=bbsb[:, :], in_=bb_d[:, :]).then_inc(s_small, 16)
            sync.dma_start(out=bo16[:, :], in_=bo_d[:, :]).then_inc(s_small, 16)
            sync.dma_start(out=bi16[:, :], in_=bi_d[:, :]).then_inc(s_small, 16)
            for j in range(NW):
                sync.dma_start(out=wosb[:, 64 * j:64 * (j + 1)],
                               in_=wo_d[128 * j:128 * (j + 1), :]
                               ).then_inc(s_wol, 16)
            sync.dma_start(out=sx[:, 0:D], in_=x_d[0:TT, :]).then_inc(s_x0, 16)
            sync.dma_start(out=sx[:, D:2 * D],
                           in_=x_d[TT:2 * TT, :]).then_inc(s_x1, 16)
            sync.wait_ge(s_cast, 1)
            sync.dma_start(out=wasb[:, :], in_=wb_d[:, :]).then_inc(s_wbl, 16)
            for j in range(NW):
                if j >= 4:
                    sync.wait_ge(s_wabs, j - 3)
                sl = j % 4
                buf = wtile if sl < 2 else vv
                so = (sl % 2) * D
                sync.dma_start(out=buf[:, so:so + D],
                               in_=wf_d[128 * j:128 * (j + 1), :]
                               ).then_inc(s_wl[sl], 16)
            for k in range(NT - 2):
                sync.wait_ge(s_xnbs, k + 1)
                sync.wait_ge(s_v, k + 1)
                sl = k % 2
                sync.dma_start(out=sx[:, sl * D:(sl + 1) * D],
                               in_=x_d[128 * (k + 2):128 * (k + 3), :]
                               ).then_inc(s_x[sl], 16)

        @block.scalar
        def _(scalar):
            scalar.wait_ge(s_small, N_SMALL)
            scalar.copy(wabb[:, :], wasb[:, :]).then_inc(s_cast, 1)
            scalar.wait_ge(s_wabtr, 1)
            scalar.drain()
            scalar.copy(
                wab[:, :].rearrange("p (c s) -> p c s", c=NCH)[:, :, 0:64],
                ptr[:, 0:1024].rearrange("p (c s) -> p c s", c=NCH),
            ).then_inc(s_wabcp, 1)
            scalar.wait_ge(s_wbl, 16)
            scalar.drain()
            scalar.copy(wabb[:, :], wasb[:, :]).then_inc(s_cast, 1)
            scalar.wait_ge(s_wabtr, 2)
            scalar.drain()
            scalar.copy(
                wab[:, :].rearrange("p (c s) -> p c s", c=NCH)[:, :, 64:128],
                ptr[:, 1024:2048].rearrange("p (c s) -> p c s", c=NCH),
            ).then_inc(s_wabcp, 1)
            scalar.copy(babr[0:1, 0:64], basb[:, :])
            scalar.copy(babr[0:1, 64:128], bbsb[:, :]).then_inc(s_bab, 1)
            scalar.wait_ge(s_wol, 16 * NW)
            scalar.drain()
            scalar.copy(wob[:, :], wosb[:, :]).then_inc(s_woc, 1)
            scalar.wait_ge(s_wotr, 1)
            scalar.drain()
            scalar.activation(wocv[0:64, :], ptr[0:64, :], AF.Identity,
                              bias=0.0, scale=-1.0).then_inc(s_wocp, 1)
            for j in range(NW):
                sl4 = j % 4
                buf = wtile if sl4 < 2 else vv
                so = (sl4 % 2) * D
                scalar.wait_ge(s_wl[sl4], 16 * (j // 4 + 1))
                if j >= 2:
                    scalar.wait_ge(s_pam, j - 1)
                scalar.drain()
                scalar.activation(xnT[:, (j % 2) * D:(j % 2 + 1) * D],
                                  buf[:, so:so + D], AF.Abs
                                  ).then_inc(s_wabs, 1)
            scalar.wait_ge(s_bc, 5)
            scalar.wait_ge(s_thr, 1)
            scalar.drain()
            scalar.copy(hsbe[64:65, :], spare[0:1, 128:256]).then_inc(s_hrow, 1)
            for j in range(NW):
                for grp in range(2):
                    scalar.wait_ge(s_wtr, 2 * j + grp + 1)
                    scalar.drain()
                    scalar.copy(
                        w_rhs[:, :].rearrange("p (c o) -> p c o", c=NCH)
                        [:, 8 * grp:8 * grp + 8, 128 * j:128 * (j + 1)],
                        ptr[:, 1024 * grp:1024 * (grp + 1)]
                        .rearrange("p (c f) -> p c f", c=8),
                    ).then_inc(s_wtrc, 1)
            scalar.wait_ge(s_stats, 1)
            scalar.drain()
            scalar.activation(xnb[:, 0:D], sx[:, 0:D], AF.Identity,
                              bias=nmr[:, 0:1], scale=yr[:, 0:1]
                              ).then_inc(s_xnbs, 1)
            scalar.wait_ge(s_tr, 1)
            scalar.drain()
            scalar.copy(xnT[:, 0:1024], ptr[:, 0:1024]).then_inc(s_xnT, 1)
            scalar.wait_ge(s_tr, 2)
            scalar.drain()
            scalar.copy(xnT[:, 1024:2048], ptr[:, 1024:2048]).then_inc(s_xnT, 1)
            for k in range(NT):
                scalar.wait_ge(s_ab, k + 1)
                if k >= 1:
                    scalar.wait_ge(s_scan, k)
                scalar.drain()
                scalar.activation(sa[:, :], pab[0:64, :], AF.Tanh)
                scalar.activation(sb0[:, :], pab[64:128, :], AF.Tanh
                                  ).then_inc(s_tanh, 2)
                if k < NT - 1:
                    j = k + 1
                    jsl = j % 2
                    scalar.wait_ge(s_stats, j + 1)
                    if k >= 1:
                        scalar.wait_ge(s_tr, 2 * k)
                    scalar.drain()
                    scalar.activation(xnb[:, jsl * D:(jsl + 1) * D],
                                      sx[:, jsl * D:(jsl + 1) * D], AF.Identity,
                                      bias=nmr[:, jsl:jsl + 1],
                                      scale=yr[:, jsl:jsl + 1]
                                      ).then_inc(s_xnbs, 1)
                scalar.wait_ge(s_scan, k + 1)
                if k >= 1:
                    scalar.wait_ge(s_mix, 2 * k)
                scalar.drain()
                scalar.activation(hsbe[0:64, :], hs[:, :], AF.Identity,
                                  bias=0.0, scale=invs64[:, :]).then_inc(s_hsb, 1)
                if k < NT - 1:
                    j = k + 1
                    jsl = j % 2
                    scalar.wait_ge(s_tr, 2 * k + 3)
                    if k >= 1:
                        scalar.wait_ge(s_ffn, k)
                    scalar.drain()
                    scalar.copy(xnT[:, jsl * D:jsl * D + 1024],
                                ptr[:, 0:1024]).then_inc(s_xnT, 1)
                    scalar.wait_ge(s_tr, 2 * k + 4)
                    scalar.drain()
                    scalar.copy(xnT[:, jsl * D + 1024:jsl * D + 2048],
                                ptr[:, 1024:2048]).then_inc(s_xnT, 1)

        @block.vector
        def _(vector):
            vector.memset(h_in[:, :], 0.0)
            vector.memset(onescol[:, :], 1.0)
            vector.memset(onescolb[:, :], 1.0)
            vector.memset(onesrf[:, :], 1.0)
            vector.wait_ge(s_small, N_SMALL)
            vector.drain()
            vector.scalar_tensor_tensor(cneg[:, :], bo16[:, :], -1.0, bi16[:, :],
                                        OP.mult, OP.subtract).then_inc(s_cvecv, 1)
            vector.wait_ge(s_pam, 16)
            vector.drain()
            vector.tensor_copy(acc[:, 0:1], spare[:, 0:1])
            vector.drain().then_inc(s_acc, 16)
            vector.wait_ge(s_psum1, 1)
            vector.drain()
            vector.tensor_scalar(scale_s[:, :], spare[0:1, 0:1], INV_WN, None,
                                 OP.mult)
            vector.drain()
            vector.tensor_scalar(thr0[:, :], scale_s[:, :], 0.5, 0.5e-10,
                                 OP.mult, OP.add)
            vector.drain()
            vector.tensor_scalar(nthr0[:, :], thr0[:, :], -1.0, None, OP.mult)
            vector.tensor_scalar(negs0[:, :], scale_s[:, :], -1.0, None, OP.mult)
            vector.drain()
            vector.reciprocal(invs0[:, :], scale_s[:, :])
            vector.drain().then_inc(s_sclv, 1)
            vector.wait_ge(s_bc, 5)
            vector.drain()
            vector.tensor_copy(thr128[:, :], spare[:, 1:2])
            vector.tensor_copy(nthr128[:, :], spare[:, 2:3])
            vector.tensor_copy(negs128[:, :], spare[:, 3:4])
            vector.tensor_copy(invs64[:, :], spare[0:64, 4:5])
            vector.drain().then_inc(s_thr, 1)
            for j in range(NW):
                sl = j % 3
                buf = outb if sl < 2 else sqscr
                so = (sl % 2) * D if sl < 2 else 0
                vector.wait_ge(s_wlb[sl], 16 * (j // 3 + 1))
                vector.drain()
                vector.tensor_scalar(g_sb[:, :], buf[:, so:so + D],
                                     thr128[:, :], None, OP.is_ge)
                if j >= 2:
                    vector.wait_ge(s_wtr, 2 * j - 2)
                vector.drain()
                vector.scalar_tensor_tensor(wbn[:, (j % 2) * D:(j % 2 + 1) * D],
                                            buf[:, so:so + D],
                                            nthr128[:, :], g_sb[:, :],
                                            OP.is_le, OP.subtract
                                            ).then_inc(s_tern, 1)

            def sums_sqs(j):
                jsl = j % 2
                vector.wait_ge(s_x[jsl], 16 * (j // 2 + 1))
                if j >= 2:
                    vector.wait_ge(s_stats, j - 1)
                vector.drain()
                vector.reduce_sum(sums[:, jsl:jsl + 1],
                                  sx[:, jsl * D:(jsl + 1) * D], axis=AX.X)
                vector.scalar_tensor_tensor(sqscr[:, :],
                                            sx[:, jsl * D:(jsl + 1) * D], 1.0,
                                            sx[:, jsl * D:(jsl + 1) * D],
                                            OP.mult, OP.mult,
                                            accum_out=sqs[:, jsl:jsl + 1]
                                            ).then_inc(s_sums, 1)

            sums_sqs(0)
            sums_sqs(1)
            vector.wait_ge(s_wabs, 16)
            vector.wait_ge(s_stats, 1)
            vector.drain()
            vector.tensor_scalar(vv[:, 0:D], sx[:, 0:D], p1r[:, 0:1],
                                 nmr[:, 0:1],
                                 OP.mult, OP.add).then_inc(s_v, 1)
            for k in range(NT):
                ksl = k % 2
                vector.wait_ge(s_tanh, 2 * k + 2)
                if k >= 1:
                    vector.wait_ge(s_hsb, k)
                vector.drain()
                vector.tensor_tensor_scan(hs[:, :], sa[:, :], sb0[:, :],
                                          h_in[:, 0:1], OP.mult, OP.add
                                          ).then_inc(s_scan, 1)
                vector.drain()
                vector.tensor_copy(h_in[:, 0:1], hs[:, TT - 1:TT])
                if k < NT - 2:
                    sums_sqs(k + 2)
                vector.wait_ge(s_mix, 2 * k + 1)
                if k >= 2:
                    vector.wait_ge(s_st[ksl], 16 * (k // 2))
                vector.drain()
                vector.scalar_tensor_tensor(outb[:, ksl * D:ksl * D + 1024],
                                            pffn[:, 0:1024], negs128[:, :],
                                            vv[:, ksl * D:ksl * D + 1024],
                                            OP.mult, OP.add).then_inc(s_ev, 1)
                vector.wait_ge(s_mix, 2 * k + 2)
                vector.drain()
                vector.scalar_tensor_tensor(
                    outb[:, ksl * D + 1024:ksl * D + 2048],
                    pffn[:, 1024:2048], negs128[:, :],
                    vv[:, ksl * D + 1024:ksl * D + 2048],
                    OP.mult, OP.add).then_inc(s_ev, 1)
                if k < NT - 1:
                    j = k + 1
                    jsl = j % 2
                    vector.wait_ge(s_stats, j + 1)
                    vector.drain()
                    vector.tensor_scalar(vv[:, jsl * D:(jsl + 1) * D],
                                         sx[:, jsl * D:(jsl + 1) * D],
                                         p1r[:, jsl:jsl + 1],
                                         nmr[:, jsl:jsl + 1], OP.mult, OP.add
                                         ).then_inc(s_v, 1)

        @block.tensor
        def _(tensor):
            tensor.wait_ge(s_cast, 1)
            for c in range(NCH):
                ins = tensor.transpose(ptr[:, 64 * c:64 * (c + 1)],
                                       wabb[0:64, 128 * c:128 * (c + 1)],
                                       sidn[0:64, 0:64])
            ins.then_inc(s_wabtr, 1)
            tensor.wait_ge(s_cast, 2)
            tensor.wait_ge(s_wabcp, 1)
            for c in range(NCH):
                ins = tensor.transpose(ptr[:, 1024 + 64 * c:1024 + 64 * (c + 1)],
                                       wabb[0:64, 128 * c:128 * (c + 1)],
                                       sidn[0:64, 0:64])
            ins.then_inc(s_wabtr, 1)
            tensor.wait_ge(s_woc, 1)
            for j in range(NW):
                if j == 8:
                    tensor.wait_ge(s_wabcp, 2)
                ins = tensor.transpose(ptr[0:64, 128 * j:128 * (j + 1)],
                                       wob[:, 64 * j:64 * (j + 1)],
                                       sidn[:, :])
            ins.then_inc(s_wotr, 1)
            tensor.wait_ge(s_cvecv, 1)
            for j in range(NW):
                tensor.wait_ge(s_wabs, j + 1)
                for c in range(NCH):
                    ins = tensor.matmul(
                        spare[:, 0:1],
                        xnT[:, (j % 2) * D + 128 * c:(j % 2) * D + 128 * (c + 1)],
                        onescolb[:, :],
                        start=(j == 0 and c == 0),
                        stop=(j == NW - 1 and c == NCH - 1))
                ins.then_inc(s_pam, 1)
            tensor.wait_ge(s_acc, 16)
            tensor.matmul(spare[0:1, 0:1], acc[:, 0:1], onescol[:, :],
                          start=True, stop=True).then_inc(s_psum1, 1)
            tensor.wait_ge(s_sclv, 1)
            tensor.matmul(spare[:, 1:2], onesrf[:, :], thr0[:, :],
                          start=True, stop=True)
            tensor.matmul(spare[:, 2:3], onesrf[:, :], nthr0[:, :],
                          start=True, stop=True)
            tensor.matmul(spare[:, 3:4], onesrf[:, :], negs0[:, :],
                          start=True, stop=True)
            tensor.matmul(spare[0:64, 4:5], onesrf[0:1, 0:64], invs0[:, :],
                          start=True, stop=True)
            tensor.matmul(spare[0:1, 128:256], invs0[:, :], onesrf[:, :],
                          start=True, stop=True).then_inc(s_bc, 5)
            for j in range(NW):
                sl = j % 2
                for grp in range(2):
                    tensor.wait_ge(s_tern, j + 1)
                    if j == 0:
                        tensor.wait_ge(s_wocp, 1)
                    else:
                        tensor.wait_ge(s_wtrc, 2 * (j - 1) + grp + 1)
                    for ci in range(8):
                        c = 8 * grp + ci
                        ins = tensor.transpose(
                            ptr[:, 128 * c:128 * (c + 1)],
                            wbn[:, sl * D + 128 * c:sl * D + 128 * (c + 1)],
                            sidn[:, :])
                    ins.then_inc(s_wtr, 1)
            tensor.wait_ge(s_wtrc, 32)
            tensor.wait_ge(s_xnbs, 1)
            for c in range(8):
                ins = tensor.transpose(ptr[:, 128 * c:128 * (c + 1)],
                                       xnb[:, 128 * c:128 * (c + 1)], sidn[:, :])
            ins.then_inc(s_tr, 1)
            for c in range(8, 16):
                ins = tensor.transpose(ptr[:, 128 * c:128 * (c + 1)],
                                       xnb[:, 128 * c:128 * (c + 1)], sidn[:, :])
            ins.then_inc(s_tr, 1)
            for k in range(NT):
                ksl = k % 2
                xo = ksl * D
                if k == 0:
                    tensor.wait_ge(s_wtrc, 32)
                else:
                    tensor.wait_ge(s_ev, 2 * k - 1)
                tensor.wait_ge(s_xnT, 2 * k + 1)
                for c in range(NCH):
                    if c == 8:
                        tensor.wait_ge(s_xnT, 2 * k + 2)
                    st_ = (c == 0)
                    lhs = xnT[:, xo + 128 * c:xo + 128 * (c + 1)]
                    tensor.matmul(pffn[:, 0:512],
                                  lhs, w_rhs[:, 2048 * c:2048 * c + 512],
                                  start=st_, stop=False)
                    tensor.matmul(pffn[:, 512:1024],
                                  lhs, w_rhs[:, 2048 * c + 512:2048 * c + 1024],
                                  start=st_, stop=False)
                if k >= 1:
                    tensor.wait_ge(s_tanh, 2 * k)
                else:
                    tensor.wait_ge(s_bab, 1)
                    tensor.wait_ge(s_wabcp, 2)
                    tensor.wait_ge(s_small, N_SMALL)
                tensor.matmul(pab[:, :], babr[:, :], sonr[:, :],
                              start=True, stop=False)
                for c in range(NCH):
                    ins = tensor.matmul(pab[:, :],
                                        wab[:, 128 * c:128 * (c + 1)],
                                        xnT[:, xo + 128 * c:xo + 128 * (c + 1)],
                                        start=False, stop=(c == NCH - 1))
                ins.then_inc(s_ab, 1)
                if k >= 1:
                    tensor.wait_ge(s_ev, 2 * k)
                for c in range(NCH):
                    st_ = (c == 0)
                    lhs = xnT[:, xo + 128 * c:xo + 128 * (c + 1)]
                    tensor.matmul(pffn[:, 1024:1536],
                                  lhs,
                                  w_rhs[:, 2048 * c + 1024:2048 * c + 1536],
                                  start=st_, stop=False)
                    ins = tensor.matmul(
                        pffn[:, 1536:2048], lhs,
                        w_rhs[:, 2048 * c + 1536:2048 * c + 2048],
                        start=st_, stop=False)
                ins.then_inc(s_ffn, 1)
                if k < NT - 1:
                    j = k + 1
                    jo = (j % 2) * D
                    tensor.wait_ge(s_xnbs, j + 1)
                    for c in range(8):
                        ins = tensor.transpose(
                            ptr[:, 128 * c:128 * (c + 1)],
                            xnb[:, jo + 128 * c:jo + 128 * (c + 1)], sidn[:, :])
                    ins.then_inc(s_tr, 1)
                    for c in range(8, 16):
                        ins = tensor.transpose(
                            ptr[:, 128 * c:128 * (c + 1)],
                            xnb[:, jo + 128 * c:jo + 128 * (c + 1)], sidn[:, :])
                    ins.then_inc(s_tr, 1)
                tensor.wait_ge(s_hsb, k + 1)
                if k == 0:
                    tensor.wait_ge(s_wocp, 1)
                    tensor.wait_ge(s_cvr, 16 * 16)
                    tensor.wait_ge(s_hrow, 1)
                tensor.matmul(pffn[:, 0:512], hsbe[:, :], wocv[:, 0:512],
                              start=False, stop=True)
                tensor.matmul(pffn[:, 512:1024], hsbe[:, :], wocv[:, 512:1024],
                              start=False, stop=True).then_inc(s_mix, 1)
                tensor.matmul(pffn[:, 1024:1536], hsbe[:, :], wocv[:, 1024:1536],
                              start=False, stop=True)
                tensor.matmul(pffn[:, 1536:2048], hsbe[:, :], wocv[:, 1536:2048],
                              start=False, stop=True).then_inc(s_mix, 1)

        @block.gpsimd
        def _(g):
            g.memset(magic[:, :], MAGIC)
            g.memset(two_i[:, :], 2)
            for j in range(NW):
                if j >= 3:
                    g.wait_ge(s_tern, j - 2)
                sl = j % 3
                buf = outb if sl < 2 else sqscr
                so = (sl % 2) * D if sl < 2 else 0
                g.dma_start(out=buf[:, so:so + D],
                            in_=wf_d[128 * j:128 * (j + 1), :]
                            ).then_inc(s_wlb[sl], 16)
            g.wait_ge(s_cvecv, 1)
            for i in range(16):
                g.dma_start(out=wocv[64:65, 128 * i:128 * (i + 1)],
                            in_=cneg[i:i + 1, :]).then_inc(s_cvr, 16)

            def chain(j):
                jsl = j % 2
                D_ = g.drain
                g.wait_ge(s_sums, j + 1)
                if j >= 2:
                    g.wait_ge(s_xnbs, j - 1)
                    g.wait_ge(s_v, j - 1)
                D_()
                g.tensor_scalar(mm_[:, :], sums[:, jsl:jsl + 1], -INV_D, None,
                                OP.mult)
                D_()
                g.tensor_tensor(msq[:, :], mm_[:, :], mm_[:, :], OP.mult)
                D_()
                g.tensor_scalar(msq[:, :], msq[:, :], -1e-5, None, OP.add)
                D_()
                g.tensor_scalar(uu[:, :], sqs[:, jsl:jsl + 1], INV_D, None,
                                OP.mult)
                D_()
                g.tensor_tensor(uu[:, :], uu[:, :], msq[:, :], OP.subtract)
                D_()
                g.tensor_tensor(shi[:, :], uu[:, :].bitcast(I32), two_i[:, :],
                                OP.divide)
                D_()
                g.tensor_tensor(yr[:, jsl:jsl + 1].bitcast(I32), magic[:, :],
                                shi[:, :], OP.subtract)
                for _i in range(3):
                    D_()
                    g.tensor_tensor(t0s[:, :], yr[:, jsl:jsl + 1],
                                    yr[:, jsl:jsl + 1], OP.mult)
                    D_()
                    g.tensor_tensor(t0s[:, :], t0s[:, :], uu[:, :], OP.mult)
                    D_()
                    g.tensor_scalar(t0s[:, :], t0s[:, :], -0.5, 1.5,
                                    OP.mult, OP.add)
                    D_()
                    g.tensor_tensor(yr[:, jsl:jsl + 1], yr[:, jsl:jsl + 1],
                                    t0s[:, :], OP.mult)
                D_()
                g.tensor_scalar(p1r[:, jsl:jsl + 1], yr[:, jsl:jsl + 1], 1.0,
                                None, OP.add)
                g.tensor_tensor(nmr[:, jsl:jsl + 1], mm_[:, :],
                                yr[:, jsl:jsl + 1], OP.mult)
                D_().then_inc(s_stats, 1)

            chain(0)
            chain(1)
            for k in range(NT):
                if k < NT - 2:
                    chain(k + 2)
                ksl = k % 2
                g.wait_ge(s_ev, 2 * k + 2)
                g.dma_start(out=y_d[128 * k:128 * (k + 1), :],
                            in_=outb[:, ksl * D:(ksl + 1) * D]
                            ).then_inc(s_st[ksl], 16)
            g.wait_ge(s_st0, 16 * ((NT + 1) // 2))
            g.wait_ge(s_st1, 16 * (NT // 2))

    return nc


_CACHE = {}


def kernel(L0_attn_x, L0_attn_Wa, L0_attn_ba, L0_attn_Wb, L0_attn_bb,
           L0_attn_Wout, L0_attn_bout, L0_ffn_weight, L0_ffn_bias):
    x = np.ascontiguousarray(np.asarray(L0_attn_x, dtype=np.float32))
    wa = np.ascontiguousarray(np.asarray(L0_attn_Wa, dtype=np.float32))
    wb = np.ascontiguousarray(np.asarray(L0_attn_Wb, dtype=np.float32))
    ba = np.asarray(L0_attn_ba, dtype=np.float32).reshape(1, S)
    bb = np.asarray(L0_attn_bb, dtype=np.float32).reshape(1, S)
    wo = np.ascontiguousarray(np.asarray(L0_attn_Wout, dtype=np.float32))
    bo = np.asarray(L0_attn_bout, dtype=np.float32).reshape(16, 128)
    wf = np.ascontiguousarray(np.asarray(L0_ffn_weight, dtype=np.float32))
    bi = np.asarray(L0_ffn_bias, dtype=np.float32).reshape(16, 128)

    if "nc" not in _CACHE:
        _CACHE["nc"] = build()
    nc = _CACHE["nc"]

    idn = np.eye(128, dtype=np.float32).astype(ml_dtypes.bfloat16)
    onesr = np.ones((1, 128), dtype=ml_dtypes.bfloat16)
    shared = dict(wa=wa, wb=wb, ba=ba, bb=bb, wout=wo, bout16=bo, bias16=bi,
                  wf=wf, idn=idn, onesr=onesr)
    in_maps = [dict(shared, x=np.ascontiguousarray(x[b])) for b in range(8)]
    res = run_bass_kernel_spmd(nc, in_maps, core_ids=list(range(8)))
    out = np.stack([np.asarray(res.results[b]["y"]) for b in range(8)], axis=0)
    return out.astype(np.float32, copy=False)


# version 16
# speedup vs baseline: 1.0438x; 1.0438x over previous
import numpy as np
import ml_dtypes
from contextlib import ExitStack

import concourse.bass as bass
import concourse.mybir as mybir
from concourse.bass_utils import run_bass_kernel_spmd

F32 = mybir.dt.float32
BF16 = mybir.dt.bfloat16
I32 = mybir.dt.int32
AF = mybir.ActivationFunctionType
OP = mybir.AluOpType
AX = mybir.AxisListType

T, D, S = 4096, 2048, 64
TT = 128
NT = T // TT
NCH = D // 128
NW = D // 128
MAGIC = 0x5F3759DF
INV_D = 1.0 / D
INV_WN = 1.0 / (D * D)


def build():
    nc = bass.Bass("TRN2", target_bir_lowering=False)
    x_d = nc.dram_tensor("x", [T, D], F32, kind="ExternalInput").ap()
    wa_d = nc.dram_tensor("wa", [S, D], F32, kind="ExternalInput").ap()
    wb_d = nc.dram_tensor("wb", [S, D], F32, kind="ExternalInput").ap()
    ba_d = nc.dram_tensor("ba", [1, S], F32, kind="ExternalInput").ap()
    bb_d = nc.dram_tensor("bb", [1, S], F32, kind="ExternalInput").ap()
    wo_d = nc.dram_tensor("wout", [D, S], F32, kind="ExternalInput").ap()
    bo_d = nc.dram_tensor("bout16", [16, 128], F32, kind="ExternalInput").ap()
    bi_d = nc.dram_tensor("bias16", [16, 128], F32, kind="ExternalInput").ap()
    wf_d = nc.dram_tensor("wf", [D, D], F32, kind="ExternalInput").ap()
    idn_d = nc.dram_tensor("idn", [128, 128], BF16, kind="ExternalInput").ap()
    onr_d = nc.dram_tensor("onesr", [1, 128], BF16, kind="ExternalInput").ap()
    y_d = nc.dram_tensor("y", [T, D], F32, kind="ExternalOutput").ap()

    with ExitStack() as est:
        e = est.enter_context

        def sb(name, shape, dt):
            return e(nc.sbuf_tensor(name, shape, dt))

        def sem(name):
            return e(nc.semaphore(name))

        w_rhs = sb("w_rhs", [128, NCH * D], BF16)
        wab = sb("wab", [128, NCH * 128], BF16)
        wocv = sb("wocv", [65, D], BF16)
        hsbe = sb("hsbe", [65, TT], BF16)
        babr = sb("babr", [1, 128], BF16)
        sidn = sb("sidn", [128, 128], BF16)
        sonr = sb("sonr", [1, 128], BF16)
        sx = sb("sx", [128, 2 * D], F32)
        xnb = sb("xnb", [128, 2 * D], BF16)
        xnT = sb("xnT", [128, 2 * D], BF16)
        outb = sb("outb", [128, 2 * D], F32)
        vv = sb("vv", [128, 2 * D], F32)
        sqscr = sb("sqscr", [128, D], F32)
        sa = sb("sa", [64, TT], F32)
        sb0 = sb("sb0", [64, TT], F32)
        hs = sb("hs", [64, TT], F32)
        h_in = sb("h_in", [64, 1], F32)
        sums = sb("sums", [128, 2], F32)
        sqs = sb("sqs", [128, 2], F32)
        mm_ = sb("mm_", [128, 1], F32)
        msq = sb("msq", [128, 1], F32)
        uu = sb("uu", [128, 1], F32)
        t0s = sb("t0s", [128, 1], F32)
        yr = sb("yr", [128, 2], F32)
        p1r = sb("p1r", [128, 2], F32)
        nmr = sb("nmr", [128, 2], F32)
        magic = sb("magic", [128, 1], I32)
        two_i = sb("two_i", [128, 1], I32)
        shi = sb("shi", [128, 1], I32)
        thr128 = sb("thr128", [128, 1], F32)
        nthr128 = sb("nthr128", [128, 1], F32)
        negs128 = sb("negs128", [128, 1], F32)
        invs64 = sb("invs64", [64, 1], F32)
        onescol = sb("onescol", [128, 1], F32)
        onescolb = sb("onescolb", [128, 1], BF16)
        onesrf = sb("onesrf", [1, 128], F32)
        scale_s = sb("scale_s", [1, 1], F32)
        thr0 = sb("thr0", [1, 1], F32)
        nthr0 = sb("nthr0", [1, 1], F32)
        negs0 = sb("negs0", [1, 1], F32)
        invs0 = sb("invs0", [1, 1], F32)
        wasb = sb("wasb", [64, D], F32)
        wabb = sb("wabb", [64, D], BF16)
        basb = sb("basb", [1, S], F32)
        bbsb = sb("bbsb", [1, S], F32)
        wosb = sb("wosb", [128, NW * 64], F32)
        wob = sb("wob", [128, NW * 64], BF16)
        bo16 = sb("bo16", [16, 128], F32)
        bi16 = sb("bi16", [16, 128], F32)
        cneg = sb("cneg", [16, 128], F32)
        wtile = sb("wtile", [128, 2 * D], F32)
        wpart = sb("wpart", [128, 16], F32)
        acc = sb("acc", [128, 2], F32)
        g_sb = sb("g_sb", [128, D], F32)
        wbn = sb("wbn", [128, 2 * D], BF16)
        pffn = e(nc.psum_tensor("pffn", [128, D], F32))
        ptr = e(nc.psum_tensor("ptr", [128, D], BF16))
        pab = e(nc.psum_tensor("pab", [128, 128], F32))
        spare = e(nc.psum_tensor("spare", [128, 512], F32))
        s_x0, s_x1 = sem("s_x0"), sem("s_x1")
        s_st0, s_st1 = sem("s_st0"), sem("s_st1")
        s_wl0, s_wl1 = sem("s_wl0"), sem("s_wl1")
        s_wlb0, s_wlb1 = sem("s_wlb0"), sem("s_wlb1")
        s_r = [sem(f"s_r{i}") for i in range(8)]
        s_q = [sem(f"s_q{i}") for i in range(8)]
        s_wl2, s_wl3 = sem("s_wl2"), sem("s_wl3")
        s_wlb2 = sem("s_wlb2")
        s_sgn2 = sem("s_sgn2")
        s_pam = sem("s_pam")
        s_acp = sem("s_acp")
        s_small = sem("s_small")
        s_wbl = sem("s_wbl")
        s_wol = sem("s_wol")
        s_cvr = sem("s_cvr")
        s_cast = sem("s_cast")
        s_wabtr = sem("s_wabtr")
        s_wabcp = sem("s_wabcp")
        s_bab = sem("s_bab")
        s_woc = sem("s_woc")
        s_wotr = sem("s_wotr")
        s_wocp = sem("s_wocp")
        s_wabs = sem("s_wabs")
        s_acc = sem("s_acc")
        s_psum1 = sem("s_psum1")
        s_sclv = sem("s_sclv")
        s_bc = sem("s_bc")
        s_thr = sem("s_thr")
        s_cvecv = sem("s_cvecv")
        s_tern = sem("s_tern")
        s_wtr = sem("s_wtr")
        s_wtrc = sem("s_wtrc")
        s_hrow = sem("s_hrow")
        s_sums = sem("s_sums")
        s_stats = sem("s_stats")
        s_xnbs = sem("s_xnbs")
        s_tr = sem("s_tr")
        s_xnT = sem("s_xnT")
        s_ab = sem("s_ab")
        s_tanh = sem("s_tanh")
        s_scan = sem("s_scan")
        s_hsb = sem("s_hsb")
        s_mix = sem("s_mix")
        s_ev = sem("s_ev")
        s_ffn = sem("s_ffn")
        s_v = sem("s_v")
        s_x = [s_x0, s_x1]
        s_st = [s_st0, s_st1]
        s_wl = [s_wl0, s_wl1, s_wl2, s_wl3]
        s_wlb = [s_wlb0, s_wlb1, s_wlb2]
        block = e(nc.Block())

        N_SMALL = 7 * 16
        RES = [(outb, 0), (outb, D), (sqscr, 0), (g_sb, 0),
               (vv, 0), (vv, D), (wtile, 0), (wtile, D)]
        ORDER = list(range(8, NW)) + list(range(8))

        @block.sync
        def _(sync):
            sync.dma_start(out=sidn[:, :], in_=idn_d[:, :]).then_inc(s_small, 16)
            sync.dma_start(out=sonr[:, :], in_=onr_d[:, :]).then_inc(s_small, 16)
            sync.dma_start(out=wasb[:, :], in_=wa_d[:, :]).then_inc(s_small, 16)
            sync.dma_start(out=basb[:, :], in_=ba_d[:, :]).then_inc(s_small, 16)
            sync.dma_start(out=bbsb[:, :], in_=bb_d[:, :]).then_inc(s_small, 16)
            sync.dma_start(out=bo16[:, :], in_=bo_d[:, :]).then_inc(s_small, 16)
            sync.dma_start(out=bi16[:, :], in_=bi_d[:, :]).then_inc(s_small, 16)
            sync.dma_start(out=sx[:, 0:D], in_=x_d[0:TT, :]).then_inc(s_x0, 16)
            sync.dma_start(out=sx[:, D:2 * D],
                           in_=x_d[TT:2 * TT, :]).then_inc(s_x1, 16)
            for j in range(8):
                if j >= 4:
                    sync.wait_ge(s_wabs, j - 3)
                sl = j % 4
                buf = wtile if sl < 2 else vv
                so = (sl % 2) * D
                sync.dma_start(out=buf[:, so:so + D],
                               in_=wf_d[128 * j:128 * (j + 1), :]
                               ).then_inc(s_wl[sl], 16)
            for j in range(8, NW):
                buf, so = RES[j - 8]
                gate = {12: 7, 13: 8, 14: 5, 15: 6}.get(j)
                if gate is not None:
                    sync.wait_ge(s_wabs, gate)
                sync.dma_start(out=buf[:, so:so + D],
                               in_=wf_d[128 * j:128 * (j + 1), :]
                               ).then_inc(s_r[j - 8], 16)
            sync.dma_start(
                out=wosb[:, :].rearrange("p (j s) -> p j s", j=NW),
                in_=wo_d[:, :].rearrange("(j p) s -> p j s", p=128)
            ).then_inc(s_wol, 16)
            sync.wait_ge(s_cast, 1)
            sync.dma_start(out=wasb[:, :], in_=wb_d[:, :]).then_inc(s_wbl, 16)
            for k in range(NT - 2):
                sync.wait_ge(s_xnbs, k + 1)
                sync.wait_ge(s_v, k + 1)
                sl = k % 2
                sync.dma_start(out=sx[:, sl * D:(sl + 1) * D],
                               in_=x_d[128 * (k + 2):128 * (k + 3), :]
                               ).then_inc(s_x[sl], 16)

        @block.scalar
        def _(scalar):
            for j in range(NW):
                if j < 8:
                    sl4 = j % 4
                    buf = wtile if sl4 < 2 else vv
                    so = (sl4 % 2) * D
                    scalar.wait_ge(s_wl[sl4], 16 * (j // 4 + 1))
                else:
                    buf, so = RES[j - 8]
                    scalar.wait_ge(s_r[j - 8], 16)
                if j >= 2:
                    scalar.wait_ge(s_pam, j - 1)
                scalar.drain()
                scalar.activation(xnT[:, (j % 2) * D:(j % 2 + 1) * D],
                                  buf[:, so:so + D], AF.Abs
                                  ).then_inc(s_wabs, 1)
            scalar.wait_ge(s_small, N_SMALL)
            scalar.copy(wabb[:, :], wasb[:, :]).then_inc(s_cast, 1)
            scalar.wait_ge(s_wabtr, 1)
            scalar.drain()
            scalar.copy(
                wab[:, :].rearrange("p (c s) -> p c s", c=NCH)[:, :, 0:64],
                ptr[:, 0:1024].rearrange("p (c s) -> p c s", c=NCH),
            ).then_inc(s_wabcp, 1)
            scalar.wait_ge(s_wbl, 16)
            scalar.drain()
            scalar.copy(wabb[:, :], wasb[:, :]).then_inc(s_cast, 1)
            scalar.wait_ge(s_wabtr, 2)
            scalar.drain()
            scalar.copy(
                wab[:, :].rearrange("p (c s) -> p c s", c=NCH)[:, :, 64:128],
                ptr[:, 1024:2048].rearrange("p (c s) -> p c s", c=NCH),
            ).then_inc(s_wabcp, 1)
            scalar.copy(babr[0:1, 0:64], basb[:, :])
            scalar.copy(babr[0:1, 64:128], bbsb[:, :]).then_inc(s_bab, 1)
            scalar.wait_ge(s_wol, 16)
            scalar.drain()
            scalar.copy(wob[:, :], wosb[:, :]).then_inc(s_woc, 1)
            scalar.wait_ge(s_wotr, 1)
            scalar.drain()
            scalar.activation(wocv[0:64, :], ptr[0:64, :], AF.Identity,
                              bias=0.0, scale=-1.0).then_inc(s_wocp, 1)
            scalar.wait_ge(s_bc, 5)
            scalar.wait_ge(s_thr, 1)
            scalar.drain()
            scalar.copy(hsbe[64:65, :], spare[0:1, 128:256]).then_inc(s_hrow, 1)
            for pos in range(NW):
                j = ORDER[pos]
                for grp in range(2):
                    scalar.wait_ge(s_wtr, 2 * pos + grp + 1)
                    scalar.drain()
                    scalar.copy(
                        w_rhs[:, :].rearrange("p (c o) -> p c o", c=NCH)
                        [:, 8 * grp:8 * grp + 8, 128 * j:128 * (j + 1)],
                        ptr[:, 1024 * grp:1024 * (grp + 1)]
                        .rearrange("p (c f) -> p c f", c=8),
                    ).then_inc(s_wtrc, 1)
            scalar.wait_ge(s_stats, 1)
            scalar.drain()
            scalar.activation(xnb[:, 0:D], sx[:, 0:D], AF.Identity,
                              bias=nmr[:, 0:1], scale=yr[:, 0:1]
                              ).then_inc(s_xnbs, 1)
            scalar.wait_ge(s_tr, 1)
            scalar.drain()
            scalar.copy(xnT[:, 0:1024], ptr[:, 0:1024]).then_inc(s_xnT, 1)
            scalar.wait_ge(s_tr, 2)
            scalar.drain()
            scalar.copy(xnT[:, 1024:2048], ptr[:, 1024:2048]).then_inc(s_xnT, 1)
            for k in range(NT):
                scalar.wait_ge(s_ab, k + 1)
                if k >= 1:
                    scalar.wait_ge(s_scan, k)
                scalar.drain()
                scalar.activation(sa[:, :], pab[0:64, :], AF.Tanh)
                scalar.activation(sb0[:, :], pab[64:128, :], AF.Tanh
                                  ).then_inc(s_tanh, 2)
                if k < NT - 1:
                    j = k + 1
                    jsl = j % 2
                    scalar.wait_ge(s_stats, j + 1)
                    if k >= 1:
                        scalar.wait_ge(s_tr, 2 * k)
                    scalar.drain()
                    scalar.activation(xnb[:, jsl * D:(jsl + 1) * D],
                                      sx[:, jsl * D:(jsl + 1) * D], AF.Identity,
                                      bias=nmr[:, jsl:jsl + 1],
                                      scale=yr[:, jsl:jsl + 1]
                                      ).then_inc(s_xnbs, 1)
                scalar.wait_ge(s_scan, k + 1)
                if k >= 1:
                    scalar.wait_ge(s_mix, 2 * k)
                scalar.drain()
                scalar.activation(hsbe[0:64, :], hs[:, :], AF.Identity,
                                  bias=0.0, scale=invs64[:, :]).then_inc(s_hsb, 1)
                if k < NT - 1:
                    j = k + 1
                    jsl = j % 2
                    scalar.wait_ge(s_tr, 2 * k + 3)
                    if k >= 1:
                        scalar.wait_ge(s_ffn, k)
                    scalar.drain()
                    scalar.copy(xnT[:, jsl * D:jsl * D + 1024],
                                ptr[:, 0:1024]).then_inc(s_xnT, 1)
                    scalar.wait_ge(s_tr, 2 * k + 4)
                    scalar.drain()
                    scalar.copy(xnT[:, jsl * D + 1024:jsl * D + 2048],
                                ptr[:, 1024:2048]).then_inc(s_xnT, 1)

        @block.vector
        def _(vector):
            vector.memset(h_in[:, :], 0.0)
            vector.memset(onescol[:, :], 1.0)
            vector.memset(onescolb[:, :], 1.0)
            vector.memset(onesrf[:, :], 1.0)
            vector.wait_ge(s_small, N_SMALL)
            vector.drain()
            vector.scalar_tensor_tensor(cneg[:, :], bo16[:, :], -1.0, bi16[:, :],
                                        OP.mult, OP.subtract).then_inc(s_cvecv, 1)
            vector.wait_ge(s_pam, 16)
            vector.drain()
            vector.tensor_copy(acc[:, 0:1], spare[:, 0:1])
            vector.drain().then_inc(s_acc, 16)
            vector.wait_ge(s_psum1, 1)
            vector.drain()
            vector.tensor_scalar(scale_s[:, :], spare[0:1, 0:1], INV_WN, None,
                                 OP.mult)
            vector.drain()
            vector.tensor_scalar(thr0[:, :], scale_s[:, :], 0.5, 0.5e-10,
                                 OP.mult, OP.add)
            vector.drain()
            vector.tensor_scalar(nthr0[:, :], thr0[:, :], -1.0, None, OP.mult)
            vector.tensor_scalar(negs0[:, :], scale_s[:, :], -1.0, None, OP.mult)
            vector.drain()
            vector.reciprocal(invs0[:, :], scale_s[:, :])
            vector.drain().then_inc(s_sclv, 1)
            vector.wait_ge(s_bc, 5)
            vector.drain()
            vector.tensor_copy(thr128[:, :], spare[:, 1:2])
            vector.tensor_copy(nthr128[:, :], spare[:, 2:3])
            vector.tensor_copy(negs128[:, :], spare[:, 3:4])
            vector.tensor_copy(invs64[:, :], spare[0:64, 4:5])
            vector.drain().then_inc(s_thr, 1)
            gv = xnT[:, :].bitcast(F32)
            for pos, j in enumerate(ORDER):
                if j >= 8:
                    buf, so = RES[j - 8]
                else:
                    buf, so = RES[j]
                    vector.wait_ge(s_q[j], 16)
                vector.drain()
                vector.tensor_scalar(gv[:, :], buf[:, so:so + D],
                                     thr128[:, :], None, OP.is_ge)
                if pos >= 2:
                    vector.wait_ge(s_wtr, 2 * pos - 2)
                vector.drain()
                vector.scalar_tensor_tensor(
                    wbn[:, (pos % 2) * D:(pos % 2 + 1) * D],
                    buf[:, so:so + D], nthr128[:, :], gv[:, :],
                    OP.is_le, OP.subtract).then_inc(s_tern, 1)

            def sums_sqs(j):
                jsl = j % 2
                vector.wait_ge(s_x[jsl], 16 * (j // 2 + 1))
                if j >= 2:
                    vector.wait_ge(s_stats, j - 1)
                vector.drain()
                vector.reduce_sum(sums[:, jsl:jsl + 1],
                                  sx[:, jsl * D:(jsl + 1) * D], axis=AX.X)
                vector.scalar_tensor_tensor(sqscr[:, :],
                                            sx[:, jsl * D:(jsl + 1) * D], 1.0,
                                            sx[:, jsl * D:(jsl + 1) * D],
                                            OP.mult, OP.mult,
                                            accum_out=sqs[:, jsl:jsl + 1]
                                            ).then_inc(s_sums, 1)

            sums_sqs(0)
            sums_sqs(1)
            vector.wait_ge(s_wabs, 16)
            vector.wait_ge(s_stats, 1)
            vector.drain()
            vector.tensor_scalar(vv[:, 0:D], sx[:, 0:D], p1r[:, 0:1],
                                 nmr[:, 0:1],
                                 OP.mult, OP.add).then_inc(s_v, 1)
            for k in range(NT):
                ksl = k % 2
                vector.wait_ge(s_tanh, 2 * k + 2)
                if k >= 1:
                    vector.wait_ge(s_hsb, k)
                vector.drain()
                vector.tensor_tensor_scan(hs[:, :], sa[:, :], sb0[:, :],
                                          h_in[:, 0:1], OP.mult, OP.add
                                          ).then_inc(s_scan, 1)
                vector.drain()
                vector.tensor_copy(h_in[:, 0:1], hs[:, TT - 1:TT])
                if k < NT - 2:
                    sums_sqs(k + 2)
                vector.wait_ge(s_mix, 2 * k + 1)
                if k >= 2:
                    vector.wait_ge(s_st[ksl], 16 * (k // 2))
                vector.drain()
                vector.scalar_tensor_tensor(outb[:, ksl * D:ksl * D + 1024],
                                            pffn[:, 0:1024], negs128[:, :],
                                            vv[:, ksl * D:ksl * D + 1024],
                                            OP.mult, OP.add).then_inc(s_ev, 1)
                vector.wait_ge(s_mix, 2 * k + 2)
                vector.drain()
                vector.scalar_tensor_tensor(
                    outb[:, ksl * D + 1024:ksl * D + 2048],
                    pffn[:, 1024:2048], negs128[:, :],
                    vv[:, ksl * D + 1024:ksl * D + 2048],
                    OP.mult, OP.add).then_inc(s_ev, 1)
                if k < NT - 1:
                    j = k + 1
                    jsl = j % 2
                    vector.wait_ge(s_stats, j + 1)
                    vector.drain()
                    vector.tensor_scalar(vv[:, jsl * D:(jsl + 1) * D],
                                         sx[:, jsl * D:(jsl + 1) * D],
                                         p1r[:, jsl:jsl + 1],
                                         nmr[:, jsl:jsl + 1], OP.mult, OP.add
                                         ).then_inc(s_v, 1)

        @block.tensor
        def _(tensor):
            tensor.wait_ge(s_cvecv, 1)
            for j in range(NW):
                tensor.wait_ge(s_wabs, j + 1)
                for c in range(NCH):
                    ins = tensor.matmul(
                        spare[:, 0:1],
                        xnT[:, (j % 2) * D + 128 * c:(j % 2) * D + 128 * (c + 1)],
                        onescolb[:, :],
                        start=(j == 0 and c == 0),
                        stop=(j == NW - 1 and c == NCH - 1))
                ins.then_inc(s_pam, 1)
            tensor.wait_ge(s_cast, 1)
            for c in range(NCH):
                ins = tensor.transpose(ptr[:, 64 * c:64 * (c + 1)],
                                       wabb[0:64, 128 * c:128 * (c + 1)],
                                       sidn[0:64, 0:64])
            ins.then_inc(s_wabtr, 1)
            tensor.wait_ge(s_cast, 2)
            tensor.wait_ge(s_wabcp, 1)
            for c in range(NCH):
                ins = tensor.transpose(ptr[:, 1024 + 64 * c:1024 + 64 * (c + 1)],
                                       wabb[0:64, 128 * c:128 * (c + 1)],
                                       sidn[0:64, 0:64])
            ins.then_inc(s_wabtr, 1)
            tensor.wait_ge(s_woc, 1)
            for j in range(NW):
                if j == 8:
                    tensor.wait_ge(s_wabcp, 2)
                ins = tensor.transpose(ptr[0:64, 128 * j:128 * (j + 1)],
                                       wob[:, 64 * j:64 * (j + 1)],
                                       sidn[:, :])
            ins.then_inc(s_wotr, 1)
            tensor.wait_ge(s_acc, 16)
            tensor.matmul(spare[0:1, 0:1], acc[:, 0:1], onescol[:, :],
                          start=True, stop=True).then_inc(s_psum1, 1)
            tensor.wait_ge(s_sclv, 1)
            tensor.matmul(spare[:, 1:2], onesrf[:, :], thr0[:, :],
                          start=True, stop=True)
            tensor.matmul(spare[:, 2:3], onesrf[:, :], nthr0[:, :],
                          start=True, stop=True)
            tensor.matmul(spare[:, 3:4], onesrf[:, :], negs0[:, :],
                          start=True, stop=True)
            tensor.matmul(spare[0:64, 4:5], onesrf[0:1, 0:64], invs0[:, :],
                          start=True, stop=True)
            tensor.matmul(spare[0:1, 128:256], invs0[:, :], onesrf[:, :],
                          start=True, stop=True).then_inc(s_bc, 5)
            for pos in range(NW):
                sl = pos % 2
                for grp in range(2):
                    tensor.wait_ge(s_tern, pos + 1)
                    if pos == 0:
                        tensor.wait_ge(s_wocp, 1)
                    else:
                        tensor.wait_ge(s_wtrc, 2 * (pos - 1) + grp + 1)
                    for ci in range(8):
                        c = 8 * grp + ci
                        ins = tensor.transpose(
                            ptr[:, 128 * c:128 * (c + 1)],
                            wbn[:, sl * D + 128 * c:sl * D + 128 * (c + 1)],
                            sidn[:, :])
                    ins.then_inc(s_wtr, 1)
            tensor.wait_ge(s_wtrc, 32)
            tensor.wait_ge(s_xnbs, 1)
            for c in range(8):
                ins = tensor.transpose(ptr[:, 128 * c:128 * (c + 1)],
                                       xnb[:, 128 * c:128 * (c + 1)], sidn[:, :])
            ins.then_inc(s_tr, 1)
            for c in range(8, 16):
                ins = tensor.transpose(ptr[:, 128 * c:128 * (c + 1)],
                                       xnb[:, 128 * c:128 * (c + 1)], sidn[:, :])
            ins.then_inc(s_tr, 1)
            for k in range(NT):
                ksl = k % 2
                xo = ksl * D
                if k >= 1:
                    tensor.wait_ge(s_tanh, 2 * k)
                else:
                    tensor.wait_ge(s_bab, 1)
                    tensor.wait_ge(s_wabcp, 2)
                    tensor.wait_ge(s_small, N_SMALL)
                tensor.wait_ge(s_xnT, 2 * k + 1)
                tensor.matmul(pab[:, :], babr[:, :], sonr[:, :],
                              start=True, stop=False)
                for c in range(NCH):
                    if c == 8:
                        tensor.wait_ge(s_xnT, 2 * k + 2)
                    ins = tensor.matmul(pab[:, :],
                                        wab[:, 128 * c:128 * (c + 1)],
                                        xnT[:, xo + 128 * c:xo + 128 * (c + 1)],
                                        start=False, stop=(c == NCH - 1))
                ins.then_inc(s_ab, 1)
                if k == 0:
                    tensor.wait_ge(s_wtrc, 32)
                else:
                    tensor.wait_ge(s_ev, 2 * k - 1)
                for c in range(NCH):
                    st_ = (c == 0)
                    lhs = xnT[:, xo + 128 * c:xo + 128 * (c + 1)]
                    tensor.matmul(pffn[:, 0:512],
                                  lhs, w_rhs[:, 2048 * c:2048 * c + 512],
                                  start=st_, stop=False)
                    tensor.matmul(pffn[:, 512:1024],
                                  lhs, w_rhs[:, 2048 * c + 512:2048 * c + 1024],
                                  start=st_, stop=False)
                if k >= 1:
                    tensor.wait_ge(s_ev, 2 * k)
                for c in range(NCH):
                    st_ = (c == 0)
                    lhs = xnT[:, xo + 128 * c:xo + 128 * (c + 1)]
                    tensor.matmul(pffn[:, 1024:1536],
                                  lhs,
                                  w_rhs[:, 2048 * c + 1024:2048 * c + 1536],
                                  start=st_, stop=False)
                    ins = tensor.matmul(
                        pffn[:, 1536:2048], lhs,
                        w_rhs[:, 2048 * c + 1536:2048 * c + 2048],
                        start=st_, stop=False)
                ins.then_inc(s_ffn, 1)
                if k < NT - 1:
                    j = k + 1
                    jo = (j % 2) * D
                    tensor.wait_ge(s_xnbs, j + 1)
                    for c in range(8):
                        ins = tensor.transpose(
                            ptr[:, 128 * c:128 * (c + 1)],
                            xnb[:, jo + 128 * c:jo + 128 * (c + 1)], sidn[:, :])
                    ins.then_inc(s_tr, 1)
                    for c in range(8, 16):
                        ins = tensor.transpose(
                            ptr[:, 128 * c:128 * (c + 1)],
                            xnb[:, jo + 128 * c:jo + 128 * (c + 1)], sidn[:, :])
                    ins.then_inc(s_tr, 1)
                tensor.wait_ge(s_hsb, k + 1)
                if k == 0:
                    tensor.wait_ge(s_wocp, 1)
                    tensor.wait_ge(s_cvr, 16 * 16)
                    tensor.wait_ge(s_hrow, 1)
                tensor.matmul(pffn[:, 0:512], hsbe[:, :], wocv[:, 0:512],
                              start=False, stop=True)
                tensor.matmul(pffn[:, 512:1024], hsbe[:, :], wocv[:, 512:1024],
                              start=False, stop=True).then_inc(s_mix, 1)
                tensor.matmul(pffn[:, 1024:1536], hsbe[:, :], wocv[:, 1024:1536],
                              start=False, stop=True)
                tensor.matmul(pffn[:, 1536:2048], hsbe[:, :], wocv[:, 1536:2048],
                              start=False, stop=True).then_inc(s_mix, 1)

        @block.gpsimd
        def _(g):
            g.memset(magic[:, :], MAGIC)
            g.memset(two_i[:, :], 2)
            for t in range(8):
                g.wait_ge(s_tern, t + 1)
                buf, so = RES[t]
                g.dma_start(out=buf[:, so:so + D],
                            in_=wf_d[128 * t:128 * (t + 1), :]
                            ).then_inc(s_q[t], 16)
            g.wait_ge(s_cvecv, 1)
            for i in range(16):
                g.dma_start(out=wocv[64:65, 128 * i:128 * (i + 1)],
                            in_=cneg[i:i + 1, :]).then_inc(s_cvr, 16)

            def chain(j):
                jsl = j % 2
                D_ = g.drain
                g.wait_ge(s_sums, j + 1)
                if j >= 2:
                    g.wait_ge(s_xnbs, j - 1)
                    g.wait_ge(s_v, j - 1)
                D_()
                g.tensor_scalar(mm_[:, :], sums[:, jsl:jsl + 1], -INV_D, None,
                                OP.mult)
                D_()
                g.tensor_tensor(msq[:, :], mm_[:, :], mm_[:, :], OP.mult)
                D_()
                g.tensor_scalar(msq[:, :], msq[:, :], -1e-5, None, OP.add)
                D_()
                g.tensor_scalar(uu[:, :], sqs[:, jsl:jsl + 1], INV_D, None,
                                OP.mult)
                D_()
                g.tensor_tensor(uu[:, :], uu[:, :], msq[:, :], OP.subtract)
                D_()
                g.tensor_tensor(shi[:, :], uu[:, :].bitcast(I32), two_i[:, :],
                                OP.divide)
                D_()
                g.tensor_tensor(yr[:, jsl:jsl + 1].bitcast(I32), magic[:, :],
                                shi[:, :], OP.subtract)
                for _i in range(3):
                    D_()
                    g.tensor_tensor(t0s[:, :], yr[:, jsl:jsl + 1],
                                    yr[:, jsl:jsl + 1], OP.mult)
                    D_()
                    g.tensor_tensor(t0s[:, :], t0s[:, :], uu[:, :], OP.mult)
                    D_()
                    g.tensor_scalar(t0s[:, :], t0s[:, :], -0.5, 1.5,
                                    OP.mult, OP.add)
                    D_()
                    g.tensor_tensor(yr[:, jsl:jsl + 1], yr[:, jsl:jsl + 1],
                                    t0s[:, :], OP.mult)
                D_()
                g.tensor_scalar(p1r[:, jsl:jsl + 1], yr[:, jsl:jsl + 1], 1.0,
                                None, OP.add)
                g.tensor_tensor(nmr[:, jsl:jsl + 1], mm_[:, :],
                                yr[:, jsl:jsl + 1], OP.mult)
                D_().then_inc(s_stats, 1)

            chain(0)
            chain(1)
            for k in range(NT):
                if k < NT - 2:
                    chain(k + 2)
                ksl = k % 2
                g.wait_ge(s_ev, 2 * k + 2)
                g.dma_start(out=y_d[128 * k:128 * (k + 1), :],
                            in_=outb[:, ksl * D:(ksl + 1) * D]
                            ).then_inc(s_st[ksl], 16)
            g.wait_ge(s_st0, 16 * ((NT + 1) // 2))
            g.wait_ge(s_st1, 16 * (NT // 2))

    return nc


_CACHE = {}


def kernel(L0_attn_x, L0_attn_Wa, L0_attn_ba, L0_attn_Wb, L0_attn_bb,
           L0_attn_Wout, L0_attn_bout, L0_ffn_weight, L0_ffn_bias):
    x = np.ascontiguousarray(np.asarray(L0_attn_x, dtype=np.float32))
    wa = np.ascontiguousarray(np.asarray(L0_attn_Wa, dtype=np.float32))
    wb = np.ascontiguousarray(np.asarray(L0_attn_Wb, dtype=np.float32))
    ba = np.asarray(L0_attn_ba, dtype=np.float32).reshape(1, S)
    bb = np.asarray(L0_attn_bb, dtype=np.float32).reshape(1, S)
    wo = np.ascontiguousarray(np.asarray(L0_attn_Wout, dtype=np.float32))
    bo = np.asarray(L0_attn_bout, dtype=np.float32).reshape(16, 128)
    wf = np.ascontiguousarray(np.asarray(L0_ffn_weight, dtype=np.float32))
    bi = np.asarray(L0_ffn_bias, dtype=np.float32).reshape(16, 128)

    if "nc" not in _CACHE:
        _CACHE["nc"] = build()
    nc = _CACHE["nc"]

    idn = np.eye(128, dtype=np.float32).astype(ml_dtypes.bfloat16)
    onesr = np.ones((1, 128), dtype=ml_dtypes.bfloat16)
    shared = dict(wa=wa, wb=wb, ba=ba, bb=bb, wout=wo, bout16=bo, bias16=bi,
                  wf=wf, idn=idn, onesr=onesr)
    in_maps = [dict(shared, x=np.ascontiguousarray(x[b])) for b in range(8)]
    res = run_bass_kernel_spmd(nc, in_maps, core_ids=list(range(8)))
    out = np.stack([np.asarray(res.results[b]["y"]) for b in range(8)], axis=0)
    return out.astype(np.float32, copy=False)
